# version 28 (final)
# baseline (speedup 1.0000x reference)
import numpy as np
import concourse.bass as bass
import concourse.mybir as mybir
from concourse.bass_utils import run_bass_kernel_spmd

F32 = mybir.dt.float32
BF16 = mybir.dt.bfloat16
AF = mybir.ActivationFunctionType
ALU = mybir.AluOpType

D = 1024
NCH = 8
NS = 256
DFF = 2736
TT = 256
ET = TT + 1
TC = 4096
NT = TC // TT
EPS_LN = 1e-5
EPS_RMS = 1e-6

FCH = [(m * 128, min(128, DFF - m * 128)) for m in range((DFF + 127) // 128)]
FGRP = []
_m = 0
while _m < len(FCH):
    g = FCH[_m:_m + 2]
    FGRP.append((g[0][0], sum(w for _, w in g), list(range(_m, _m + len(g)))))
    _m += 2


class Plan:

    NQS = 48
    NQG = 20
    NQA = 16

    def __init__(self):
        self.events = {k: [] for k in ("pe", "dv", "ac", "pl", "sy")}
        self.cnt = {"pe": 0, "dv": 0, "ac": 0, "pl": 0, "qc": 0}
        for i in range(self.NQS):
            self.cnt[f"qs{i}"] = 0
        for i in range(self.NQG):
            self.cnt[f"qg{i}"] = 0
        for i in range(self.NQA):
            self.cnt[f"qa{i}"] = 0
        self.qs_rr = 0
        self.qg_rr = 0
        self.qa_rr = 0
        self.last_w = {}
        self.last_r = {}

    def ev(self, eng, fn, reads=(), writes=(), extra=(), dma=0):
        waits = {}
        def add(sk, v):
            if v > waits.get(sk, -1):
                waits[sk] = v
        if dma == "coll":
            own = "qc"
        elif dma:
            if eng == "sy":
                own = f"qs{self.qs_rr % self.NQS}"
                self.qs_rr += 1
            elif eng == "ac":
                own = f"qa{self.qa_rr % self.NQA}"
                self.qa_rr += 1
            else:
                own = f"qg{self.qg_rr % self.NQG}"
                self.qg_rr += 1
        else:
            own = eng
        if dma:
            if self.cnt[own]:
                add(own, self.cnt[own])
        for b in reads:
            if b in self.last_w:
                sk, v = self.last_w[b]
                if sk != eng or dma:
                    if sk != own:
                        add(sk, v)
        for b in writes:
            if b in self.last_w:
                sk, v = self.last_w[b]
                if sk != own:
                    add(sk, v)
            for sk, v in self.last_r.get(b, {}).items():
                if sk != own:
                    add(sk, v)
        for sk, v in extra:
            add(sk, v)
        for sk, v in waits.items():
            assert v <= self.cnt[sk], (sk, v, self.cnt[sk])
        if dma == "coll":
            self.cnt[own] += 1
        elif dma:
            self.cnt[own] += 16 * dma
        else:
            self.cnt[own] += 1
        post = self.cnt[own]
        self.events[eng].append((sorted(waits.items()), fn, dma, post, own))
        for b in writes:
            self.last_w[b] = (own, post)
            self.last_r.pop(b, None)
        for b in reads:
            self.last_r.setdefault(b, {})[own] = post
        return post


def build_program():
    nc = bass.Bass("TRN2", target_bir_lowering=False)

    xT = nc.dram_tensor("xT", [D, TC + 1], F32, kind="ExternalInput")
    wrT = nc.dram_tensor("wrT", [D, D], F32, kind="ExternalInput")
    wkT = nc.dram_tensor("wkT", [D, D], F32, kind="ExternalInput")
    wvT = nc.dram_tensor("wvT", [D, D], F32, kind="ExternalInput")
    bwT = nc.dram_tensor("bwT", [D, NS], F32, kind="ExternalInput")
    cwT = nc.dram_tensor("cwT", [NS, D], F32, kind="ExternalInput")
    w1T = nc.dram_tensor("w1T", [D, DFF], F32, kind="ExternalInput")
    w3T = nc.dram_tensor("w3T", [D, DFF], F32, kind="ExternalInput")
    w2T = nc.dram_tensor("w2T", [DFF, D], F32, kind="ExternalInput")
    mu_r8 = nc.dram_tensor("mu_r8", [128, NCH], F32, kind="ExternalInput")
    mu_k8 = nc.dram_tensor("mu_k8", [128, NCH], F32, kind="ExternalInput")
    dsk8 = nc.dram_tensor("dsk8", [128, NCH], F32, kind="ExternalInput")
    wssm8 = nc.dram_tensor("wssm8", [128, NCH], F32, kind="ExternalInput")
    wffn8 = nc.dram_tensor("wffn8", [128, NCH], F32, kind="ExternalInput")
    loglam2 = nc.dram_tensor("loglam2", [128, 2], F32, kind="ExternalInput")
    cexpo8 = nc.dram_tensor("cexpo8", [128, 8], F32, kind="ExternalInput")
    csel8 = nc.dram_tensor("csel8", [128, 8], F32, kind="ExternalInput")
    tI128 = nc.dram_tensor("tI128", [128, TT], F32, kind="ExternalInput")
    outT = nc.dram_tensor("outT", [D, TC], F32, kind="ExternalOutput")
    w1s = nc.dram_tensor("w1s", [D, DFF], BF16)
    w3s = nc.dram_tensor("w3s", [D, DFF], BF16)
    w2s = nc.dram_tensor("w2s", [DFF, D], BF16)
    hss = nc.dram_tensor("hss", [NS, TC], F32)
    rowS = nc.dram_tensor("rowS", [NT, 3 * ET], F32)
    hloc = nc.dram_tensor("hloc", [128, 2], F32)
    hgath = nc.dram_tensor("hgath", [512, 2], F32)

    sb = nc.alloc_sbuf_tensor
    xs = [sb("xs0", [128, NCH * ET], F32), sb("xs1", [128, NCH * ET], F32)]
    xbfp = [sb("xbfp0", [128, NCH * ET], BF16), sb("xbfp1", [128, NCH * ET], BF16)]
    sqbf1 = sb("sqbf", [128, NCH * ET], BF16)
    sqbfp = [sqbf1, sqbf1]
    xn = [sb("xn0", [128, NCH * ET], F32), sb("xn1", [128, NCH * ET], F32)]
    hs = [sb("hs0", [128, 2 * TT], F32), sb("hs1", [128, 2 * TT], F32)]
    h0b = sb("h0b", [128, 2], F32)
    hsbf = [sb("hsbf0", [128, 2 * TT], BF16), sb("hsbf1", [128, 2 * TT], BF16)]
    a_sb2 = [sb("a_sb0", [128, NCH * TT], F32), sb("a_sb1", [128, NCH * TT], F32)]
    d8r = sb("d8r", [128, 2 * TT], F32)
    xrbf = sb("xrbf", [128, NCH * TT], BF16)
    xkbf = sb("xkbf", [128, NCH * TT], BF16)
    r_sb = sb("r_sb", [128, 2 * TT], BF16)
    kact = sb("kact", [128, NCH * TT], BF16)
    fring = sb("fring", [128, 2 * TT], F32)
    zbf = sb("zbf", [128, NCH * TT], BF16)
    krsc = sb("krsc", [128, TT], F32)
    gssc = sb("gssc", [128, 2 * TT], BF16)
    vssc = sb("vssc", [128, 2 * TT], BF16)
    gvbf = sb("gvbf", [128, 3 * TT], BF16)
    rubc = [sb("rubc0", [128, TT], F32), sb("rubc1", [128, TT], F32)]
    rzbc = sb("rzbc", [128, TT], F32)
    mbc_sb = sb("mbc_sb", [128, ET], F32)
    rsbc_sb = sb("rsbc_sb", [128, ET], F32)
    rsc = sb("rsc", [1, ET], F32)
    rr = [[sb(f"r{i}p{p}", [1, ET], F32) for i in range(4)] for p in range(2)]
    wr_sb = sb("wr_sb", [128, NCH * D], BF16)
    wk_sb = sb("wk_sb", [128, NCH * D], BF16)
    wv_sb = sb("wv_sb", [128, NCH * D], BF16)
    wb_sb = sb("wb_sb", [128, NCH * NS], BF16)
    cw_sb = sb("cw_sb", [128, 2 * D], BF16)
    w1g = [sb("w1g0", [128, 2048], BF16), sb("w1g1", [128, 2048], BF16)]
    w3g = [sb("w3g0", [128, 2048], BF16), sb("w3g1", [128, 2048], BF16)]
    w2r = sb("w2r", [128, 3 * D], BF16)
    lam_rep = sb("lam_rep", [128, 2 * TT], F32)
    ones1b = sb("ones1b", [128, 1], BF16)
    onesk1 = sb("onesk1", [1, 128], F32)
    mu_r_v = sb("mu_r_v", [128, NCH], F32)
    mu_k_v = sb("mu_k_v", [128, NCH], F32)
    dsk_v = sb("dsk_v", [128, NCH], F32)
    wssm_v = sb("wssm_v", [128, NCH], F32)
    wffn_v = sb("wffn_v", [128, NCH], F32)
    loglam_v = sb("loglam_v", [128, 2], F32)
    lam_v = sb("lam_v", [128, 2], F32)
    cvec = sb("cvec", [128, 2], F32)
    bu_sb1 = sb("bu_sb", [128, 2 * TT], F32)
    bu_sb = [bu_sb1, bu_sb1]
    negc = sb("negc", [1, NS], BF16)
    mbf1 = sb("mbf", [1, TT], BF16)
    mbf = [mbf1, mbf1]
    rowpack = [sb("rowpack0", [1, 3 * ET], F32), sb("rowpack1", [1, 3 * ET], F32)]
    heg = sb("heg", [128, 8], F32)
    lnl = sb("lnl", [128, 2], F32)
    lnl8 = sb("lnl8", [128, 8], F32)
    coef = sb("coef", [128, 8], F32)
    prodc = sb("prodc", [128, 8], F32)
    qcar = sb("qcar", [128, 2], F32)
    h0add = sb("h0add", [128, 2], F32)
    lam256 = sb("lam256", [128, 2], F32)
    lpl = sb("lpl", [128, 2 * TT], F32)
    hend_sb = sb("hend_sb", [128, 2], F32)
    cexpo_v = sb("cexpo_v", [128, 8], F32)
    csel_v = sb("csel_v", [128, 8], F32)
    tI_v = sb("tI_v", [128, TT], F32)

    pb = [nc.alloc_psum_tensor(f"pb{i}", [128, 512], F32) for i in range(8)]
    PBN = [f"pb{i}" for i in range(8)]

    P = Plan()
    E = lambda c: slice(ET * c, ET * c + ET)
    R = lambda c: slice(ET * c + 1, ET * c + ET)
    C = lambda c: slice(TT * c, TT * c + TT)

    def cast_dma(dst, src, rr):
        def fn(e, ic):
            if rr:
                return ic(e.dma_start(out=dst[:].rearrange("p (k m) -> p k m", k=rr),
                                      in_=src[:, :].rearrange("(k p) m -> p k m", p=128)))
            return ic(e.dma_start(out=dst[:, :], in_=src[:, :]))
        return fn
    P.ev("pl", cast_dma(cw_sb, cwT, 2), writes=["cw_sb"], dma=1)
    P.ev("sy", lambda e, ic: ic(e.dma_start(
             out=a_sb2[0][:, 0:NCH * NS].rearrange("p (k n) -> p k n", n=NS),
             in_=bwT[:, :].rearrange("(k p) n -> p k n", p=128))),
         writes=["a_sb0"], dma=1)

    def vec_loads(e, ic):
        ic(e.dma_start(out=mu_r_v[:], in_=mu_r8[:, :]))
        ic(e.dma_start(out=mu_k_v[:], in_=mu_k8[:, :]))
        ic(e.dma_start(out=dsk_v[:], in_=dsk8[:, :]))
        ic(e.dma_start(out=wssm_v[:], in_=wssm8[:, :]))
        ic(e.dma_start(out=wffn_v[:], in_=wffn8[:, :]))
        ic(e.dma_start(out=cexpo_v[:], in_=cexpo8[:, :]))
        ic(e.dma_start(out=csel_v[:], in_=csel8[:, :]))
        ic(e.dma_start(out=tI_v[:], in_=tI128[:, :]))
        return ic(e.dma_start(out=loglam_v[:], in_=loglam2[:, :]))
    P.ev("sy", vec_loads, writes=["mu_r_v", "mu_k_v", "dsk_v", "wssm_v", "wffn_v",
                                  "loglam_v", "cexpo_v", "csel_v", "tI_v"], dma=9)

    def consts(e):
        e.memset(ones1b[:], 1.0)
        e.memset(onesk1[:], 1.0)
        e.memset(h0b[:], 0.0)
        e.memset(cvec[:, 0:1], EPS_LN)
        e.memset(cvec[:, 1:2], EPS_RMS)
        return e.memset(lam_rep[:], 1.0)
    P.ev("dv", consts, writes=["ones1b", "onesk1", "h0b", "lam_rep", "cvec"])

    P.ev("ac", lambda e: e.activation(lam_v[:], loglam_v[:], AF.Sigmoid),
         reads=["loglam_v"], writes=["lam_v"])

    def lamrep(e):
        e.tensor_scalar(lam_rep[:, 0:TT], lam_rep[:, 0:TT], lam_v[:, 0:1], None, ALU.mult)
        return e.tensor_scalar(lam_rep[:, TT:2 * TT], lam_rep[:, TT:2 * TT], lam_v[:, 1:2], None, ALU.mult)
    P.ev("dv", lamrep, reads=["lam_v"], writes=["lam_rep"])

    def wbfold(e):
        for k in range(NCH):
            op = e.tensor_scalar(wb_sb[:, NS * k: NS * k + NS], a_sb2[0][:, NS * k: NS * k + NS],
                                 wssm_v[:, k:k + 1], None, ALU.mult)
        return op
    P.ev("dv", wbfold, reads=["a_sb0", "wssm_v"], writes=["wb_sb"])

    P.ev("ac", lambda e: e.activation(lnl[:], lam_v[:], AF.Ln), reads=["lam_v"], writes=["lnl"])

    def lnl8_fn(e):
        for j in range(4):
            op = e.tensor_copy(lnl8[:, 2 * j: 2 * j + 2], lnl[:])
        return op
    P.ev("dv", lnl8_fn, reads=["lnl"], writes=["lnl8"])

    def lpl_pre(e):
        e.tensor_scalar(lpl[:, 0:TT], tI_v[:], lnl[:, 0:1], None, ALU.mult)
        e.tensor_scalar(lpl[:, TT:2 * TT], tI_v[:], lnl[:, 1:2], None, ALU.mult)
        return e.tensor_scalar(lam256[:], lnl[:], float(TT), None, ALU.mult)
    P.ev("dv", lpl_pre, reads=["lnl", "tI_v"], writes=["lpl", "lam256"])

    def lpl_exp(e):
        e.activation(lpl[:], lpl[:], AF.Exp)
        e.drain()
        return e.activation(lam256[:], lam256[:], AF.Exp)
    P.ev("ac", lpl_exp, reads=[], writes=["lpl", "lam256"])

    def colsum_fn(e):
        for k in range(NCH):
            op = e.matmul(pb[0][0:1, 0:NS], ones1b[:], wb_sb[:, NS * k: NS * k + NS],
                          start=(k == 0), stop=(k == NCH - 1))
        return op
    P.ev("pe", colsum_fn, reads=["wb_sb", "ones1b"], writes=["pb0"])
    P.ev("dv", lambda e: e.tensor_scalar(negc[:], pb[0][0:1, 0:NS], -1.0, None, ALU.mult),
         reads=["pb0"], writes=["negc"])

    def xload(t):
        x = xs[t % 2]
        def fn(e, ic):
            return ic(e.dma_start(
                out=x[:].rearrange("p (c w) -> p c w", w=ET),
                in_=xT[:, TT * t: TT * t + ET].rearrange("(c p) w -> p c w", p=128)))
        return fn

    def stream_w13(b, g):
        f0, fw, _ = FGRP[g]
        def fn(e, ic):
            ic(e.dma_start(out=w1g[b][:, 0:NCH * fw].rearrange("p (k f) -> p k f", f=fw),
                           in_=w1s[:, f0:f0 + fw].rearrange("(k p) f -> p k f", p=128)))
            return ic(e.dma_start(out=w3g[b][:, 0:NCH * fw].rearrange("p (k f) -> p k f", f=fw),
                                  in_=w3s[:, f0:f0 + fw].rearrange("(k p) f -> p k f", p=128)))
        return fn

    def stream_w2(m):
        f0, fw = FCH[m]
        s = m % 3
        def fn(e, ic):
            return ic(e.dma_start(out=w2r[0:fw, D * s:D * s + D], in_=w2s[f0:f0 + fw, :]))
        return fn

    P.ev("sy", xload(0), writes=["xs0"], dma=1)
    P.ev("sy", xload(1), writes=["xs1"], dma=1)

    def s_mms(src, bank):
        def fn(e):
            for c in range(NCH):
                e.matmul(bank[0:1, 0:1], ones1b[:], src[:, ET * c: ET * c + 1],
                         start=(c == 0), stop=(c == NCH - 1))
            for c in range(NCH):
                op = e.matmul(bank[0:1, 1:ET], ones1b[:], src[:, R(c)],
                              start=(c == 0), stop=(c == NCH - 1))
            return op
        return fn

    def p0_vars(t):
        p = t % 2
        return dict(p=p, x=xs[p], xsn=f"xs{p}", rws=rr[p],
                    rn=[f"r{i}p{p}" for i in range(4)],
                    sA=0 if p == 0 else 2, sB=1 if p == 0 else 3,
                    gB=4 if p == 0 else 7, bu=bu_sb[p], bun="bu_sb",
                    ruv=rubc[p], run=f"rubc{p}", hsv=hs[p], hsn=f"hs{p}")

    def plan_p0_load(t):
        v = p0_vars(t)
        p, x, xsn = v["p"], v["x"], v["xsn"]
        sA, sB = v["sA"], v["sB"]
        xb = xbfp[p]
        xbn = f"xbfp{p}"
        sq = sqbf1
        sqn = "sqbf"
        P.ev("dv", lambda e, x=x, xb=xb: e.tensor_copy(xb[:], x[:]), reads=[xsn], writes=[xbn])
        P.ev("dv", lambda e, x=x, sq=sq: e.tensor_mul(sq[:], x[:], x[:]), reads=[xsn], writes=["sqbf"])
        if t + 2 < NT:
            P.ev("sy", xload(t + 2), writes=[xsn], dma=1)
        P.ev("pe", s_mms(xb, pb[sA]), reads=[xbn, "ones1b"], writes=[PBN[sA]])
        P.ev("pe", s_mms(sq, pb[sB]), reads=["sqbf", "ones1b"], writes=[PBN[sB]])

    def plan_p0_rest(t):
        v = p0_vars(t)
        p, x, xsn = v["p"], v["x"], v["xsn"]
        r0, r1, r2, r3 = v["rws"]
        r0n, r1n, r2n, r3n = v["rn"]
        sA, sB, gB = v["sA"], v["sB"], v["gB"]
        bu, bun, ruv, run, hsv, hsn = v["bu"], v["bun"], v["ruv"], v["run"], v["hsv"], v["hsn"]
        xb = xbfp[p]
        xbn = f"xbfp{p}"

        def rows1(e, r0=r0, r1=r1, r2=r2, sA=sA, sB=sB):
            e.tensor_scalar(r0[:], pb[sA][0:1, 0:ET], 1.0 / D, None, ALU.mult)
            e.tensor_scalar(r1[:], pb[sB][0:1, 0:ET], 1.0 / D, None, ALU.mult)
            e.drain()
            e.tensor_mul(r2[:], r0[:], r0[:])
            e.drain()
            return e.tensor_sub(r1[:], r1[:], r2[:])
        P.ev("dv", rows1, reads=[PBN[sA], PBN[sB]], writes=[r0n, r1n, r2n])

        P.ev("ac", lambda e, r1=r1, r2=r2: e.activation(r2[:], r1[:], AF.Sqrt, bias=cvec[0:1, 0:1]),
             reads=[r1n, "cvec"], writes=[r2n])
        def rows2b(e, r1=r1, r2=r2, r3=r3):
            e.reciprocal(r3[:], r2[:])
            e.drain()
            e.tensor_mul(r2[:], r3[:], r3[:])
            e.drain()
            return e.tensor_mul(r1[:], r1[:], r2[:])
        P.ev("dv", rows2b, reads=[r2n], writes=[r1n, r2n, r3n])
        P.ev("ac", lambda e, r1=r1, r2=r2: e.activation(r2[:], r1[:], AF.Sqrt, bias=cvec[0:1, 1:2]),
             reads=[r1n, "cvec"], writes=[r2n])
        P.ev("dv", lambda e, r1=r1, r2=r2: e.reciprocal(r1[:], r2[:]),
             reads=[r2n], writes=[r1n])

        def rows4(e, r0=r0, r1=r1, r2=r2, r3=r3, p=p):
            e.tensor_mul(r2[:], r3[:], r1[:])
            e.tensor_copy(mbf[p][:], r0[0:1, 1:ET])
            e.tensor_copy(rowpack[p][0:1, 0:ET], r0[:])
            e.tensor_copy(rowpack[p][0:1, ET:2 * ET], r3[:])
            e.tensor_copy(rowpack[p][0:1, 2 * ET:2 * ET + TT], r1[0:1, 1:ET])
            return e.tensor_copy(rowpack[p][0:1, 3 * ET - 1:3 * ET], r1[0:1, 0:1])
        P.ev("dv", rows4, reads=[r0n, r1n, r3n], writes=[r2n, "mbf", f"rowpack{p}"])

        P.ev("pe", lambda e, r2=r2, gB=gB: e.matmul(pb[gB][:, 0:TT], onesk1[:], r2[0:1, 1:ET],
                                                    start=True, stop=True),
             reads=[r2n, "onesk1"], writes=[PBN[gB]])
        P.ev("dv", lambda e, ruv=ruv, gB=gB: e.tensor_copy(ruv[:], pb[gB][:, 0:TT]),
             reads=[PBN[gB]], writes=[run])

        for n in range(2):
            def bu_fn(e, n=n, p=p, xb=xb):
                for k in range(NCH):
                    e.matmul(pb[5 + n][:, 0:TT], wb_sb[:, NS * k + 128 * n: NS * k + 128 * n + 128],
                             xb[:, R(k)], start=(k == 0), stop=False, skip_group_check=True)
                return e.matmul(pb[5 + n][:, 0:TT], negc[0:1, 128 * n: 128 * n + 128],
                                mbf[p][:], start=False, stop=True, skip_group_check=True)
            P.ev("pe", bu_fn, reads=[xbn, "wb_sb", "negc", "mbf"], writes=[PBN[5 + n]])
            P.ev("dv", lambda e, n=n, bu=bu, ruv=ruv: e.tensor_mul(
                     bu[:, TT * n: TT * n + TT], pb[5 + n][:, 0:TT], ruv[:]),
                 reads=[PBN[5 + n], run], writes=[bun])

        if t > 0:
            def h0copy(e, hp=hs[1 - p]):
                e.tensor_copy(h0b[:, 0:1], hp[:, TT - 1:TT])
                return e.tensor_copy(h0b[:, 1:2], hp[:, 2 * TT - 1:2 * TT])
            P.ev("dv", h0copy, reads=[f"hs{1 - p}"], writes=["h0b"])

        def scan_fn(e, bu=bu, hsv=hsv):
            e.tensor_tensor_scan(hsv[:, 0:TT], lam_rep[:, 0:TT], bu[:, 0:TT],
                                 h0b[:, 0:1], ALU.mult, ALU.add)
            return e.tensor_tensor_scan(hsv[:, TT:2 * TT], lam_rep[:, TT:2 * TT],
                                        bu[:, TT:2 * TT], h0b[:, 1:2], ALU.mult, ALU.add)
        P.ev("dv", scan_fn, reads=[bun, "lam_rep", "h0b"], writes=[hsn])

        def p0store(e, ic, t=t, hsv=hsv, p=p):
            ic(e.dma_start(out=hss[:, TT * t: TT * t + TT].rearrange("(n p) w -> p n w", p=128),
                           in_=hsv[:].rearrange("p (n w) -> p n w", w=TT)))
            return ic(e.dma_start(out=rowS[t:t + 1, :], in_=rowpack[p][:]))
        P.ev("sy", p0store, reads=[hsn, f"rowpack{p}"], writes=[f"hss{t}", f"rowS{t}"], dma=2)


    plan_p0_load(0)
    plan_p0_load(1)
    for t in range(NT):
        plan_p0_rest(t)
        if t + 2 < NT:
            plan_p0_load(t + 2)
        if t == 1:
            P.ev("pl", cast_dma(wr_sb, wrT, NCH), writes=["wr_sb"], dma=1)
            P.ev("pl", cast_dma(wk_sb, wkT, NCH), writes=["wk_sb"], dma=1)
            P.ev("pl", cast_dma(wv_sb, wvT, NCH), writes=["wv_sb"], dma=1)
            P.ev("pl", cast_dma(w1s, w1T, 0), writes=["w1s"], dma=1)
            P.ev("pl", cast_dma(w3s, w3T, 0), writes=["w3s"], dma=1)
            P.ev("pl", cast_dma(w2s, w2T, 0), writes=["w2s"], dma=1)
            P.ev("sy", stream_w13(0, 0), reads=["w1s", "w3s"], writes=["w1g0", "w3g0"], dma=2)
            P.ev("sy", stream_w13(1, 1), reads=["w1s", "w3s"], writes=["w1g1", "w3g1"], dma=2)
            for mm3 in range(3):
                P.ev("ac", stream_w2(mm3), reads=["w2s"], writes=[f"w2r{mm3 % 3}"], dma=1)

    def hend_fn(e):
        e.tensor_copy(hend_sb[:, 0:1], hs[(NT - 1) % 2][:, TT - 1:TT])
        return e.tensor_copy(hend_sb[:, 1:2], hs[(NT - 1) % 2][:, 2 * TT - 1:2 * TT])
    P.ev("dv", hend_fn, reads=[f"hs{(NT - 1) % 2}"], writes=["hend_sb"])
    P.ev("pl", lambda e, ic: ic(e.dma_start(out=hloc[:, :], in_=hend_sb[:])),
         reads=["hend_sb"], writes=["hloc"], dma=1)
    P.ev("pl", lambda e, ic: ic(e.collective_compute(
             "AllGather", ALU.bypass, replica_groups=[[0, 1, 2, 3], [4, 5, 6, 7]],
             ins=[hloc.ap().opt()], outs=[hgath.ap().opt()])),
         reads=["hloc"], writes=["hgath"], dma="coll")
    P.ev("pl", lambda e, ic: ic(e.dma_start(out=heg[:].rearrange("p (j c) -> p j c", c=2),
                                            in_=hgath[:, :].rearrange("(j p) c -> p j c", p=128))),
         reads=["hgath"], writes=["heg"], dma=1)

    def coef_fn(e):
        e.tensor_mul(coef[:], cexpo_v[:], lnl8[:])
        return e.drain()
    P.ev("dv", coef_fn, reads=["cexpo_v", "lnl8"], writes=["coef"])
    P.ev("ac", lambda e: e.activation(coef[:], coef[:], AF.Exp), reads=[], writes=["coef"])
    P.ev("dv", lambda e: e.tensor_mul(coef[:], coef[:], csel_v[:]), reads=["csel_v"], writes=["coef"])

    def h0add_fn(e):
        e.tensor_mul(prodc[:], coef[:], heg[:])
        e.drain()
        e.tensor_add(h0add[:], prodc[:, 0:2], prodc[:, 2:4])
        e.drain()
        e.tensor_add(h0add[:], h0add[:], prodc[:, 4:6])
        e.drain()
        e.tensor_add(h0add[:], h0add[:], prodc[:, 6:8])
        e.drain()
        return e.tensor_copy(qcar[:], h0add[:])
    P.ev("dv", h0add_fn, reads=["coef", "heg"], writes=["prodc", "h0add", "qcar"])

    P.ev("sy", xload(0), writes=["xs0"], dma=1)
    P.ev("sy", xload(1), writes=["xs1"], dma=1)
    def mvars(t):
        p = t % 2
        return (p, xs[p], f"xs{p}", xn[p], f"xn{p}", hs[p], f"hs{p}",
                hsbf[p], f"hsbf{p}", rubc[p], f"rubc{p}", rowpack[p], f"rowpack{p}")

    def plan_B(t):
        p, x, xsn, xnv, xnn, hsv, hsn, hbv, hbn, ruv, run, rpv, rpn = mvars(t)

        def rowload(e, ic, t=t, rpv=rpv, hsv=hsv):
            ic(e.dma_start(out=rpv[:], in_=rowS[t:t + 1, :]))
            return ic(e.dma_start(out=hsv[:].rearrange("p (n w) -> p n w", w=TT),
                                  in_=hss[:, TT * t: TT * t + TT].rearrange("(n p) w -> p n w", p=128)))
        P.ev("sy", rowload, reads=[f"hss{t}", f"rowS{t}"], writes=[rpn, hsn], dma=2)

        P.ev("pe", lambda e, rpv=rpv: (
                 e.matmul(pb[5][:, 0:1], onesk1[:], rpv[0:1, 0:1], start=True, stop=True),
                 e.matmul(pb[5][:, 1:ET], onesk1[:], rpv[0:1, 1:ET], start=True, stop=True))[-1],
             reads=[rpn, "onesk1"], writes=["pb5"])
        P.ev("dv", lambda e: e.tensor_copy(mbc_sb[:], pb[5][:, 0:ET]), reads=["pb5"], writes=["mbc_sb"])
        P.ev("pe", lambda e, rpv=rpv: (
                 e.matmul(pb[5][:, 0:1], onesk1[:], rpv[0:1, ET:ET + 1], start=True, stop=True),
                 e.matmul(pb[5][:, 1:ET], onesk1[:], rpv[0:1, ET + 1:2 * ET], start=True, stop=True))[-1],
             reads=[rpn, "onesk1"], writes=["pb5"])
        P.ev("dv", lambda e: e.tensor_copy(rsbc_sb[:], pb[5][:, 0:ET]), reads=["pb5"], writes=["rsbc_sb"])
        P.ev("pe", lambda e, rpv=rpv: e.matmul(pb[5][:, 0:TT], onesk1[:],
                                               rpv[0:1, 2 * ET:2 * ET + TT], start=True, stop=True),
             reads=[rpn, "onesk1"], writes=["pb5"])
        P.ev("dv", lambda e, ruv=ruv: e.tensor_copy(ruv[:], pb[5][:, 0:TT]),
             reads=["pb5"], writes=[run])

        def xn_fn(e, x=x, xnv=xnv):
            for c in range(NCH):
                e.tensor_sub(xnv[:, E(c)], x[:, E(c)], mbc_sb[:])
            e.drain()
            for c in range(NCH):
                op = e.tensor_mul(xnv[:, E(c)], xnv[:, E(c)], rsbc_sb[:])
            return op
        P.ev("dv", xn_fn, reads=[xsn, "mbc_sb", "rsbc_sb"], writes=[xnn])

        def abase_fn(e, xnv=xnv, ruv=ruv, av=a_sb2[p]):
            for c in range(NCH):
                sl = d8r[:, TT * (c % 2): TT * (c % 2) + TT]
                e.scalar_tensor_tensor(sl, xnv[:, R(c)], wssm_v[:, c:c + 1],
                                       ruv[:], ALU.mult, ALU.mult)
                e.drain()
                op = e.scalar_tensor_tensor(av[:, C(c)], sl, dsk_v[:, c:c + 1],
                                            xnv[:, R(c)], ALU.mult, ALU.add)
            return op
        P.ev("dv", abase_fn, reads=[xnn, run, "wssm_v", "dsk_v"], writes=[f"a_sb{p}", "d8r"])

        def mix_fn(e, xnv=xnv):
            for c in range(NCH):
                sl = d8r[:, TT * (c % 2): TT * (c % 2) + TT]
                e.tensor_sub(sl, xnv[:, R(c)], xnv[:, ET * c: ET * c + TT])
                e.drain()
                e.scalar_tensor_tensor(xrbf[:, C(c)], sl, mu_r_v[:, c:c + 1],
                                       xnv[:, ET * c: ET * c + TT], ALU.mult, ALU.add)
                op = e.scalar_tensor_tensor(xkbf[:, C(c)], sl, mu_k_v[:, c:c + 1],
                                            xnv[:, ET * c: ET * c + TT], ALU.mult, ALU.add)
            return op
        P.ev("dv", mix_fn, reads=[xnn, "mu_r_v", "mu_k_v"], writes=["xrbf", "xkbf", "d8r"])

        def hscorr(e, hsv=hsv):
            for n in range(2):
                e.tensor_scalar(d8r[:, TT * n: TT * n + TT], lpl[:, TT * n: TT * n + TT],
                                qcar[:, n:n + 1], None, ALU.mult)
            e.drain()
            for n in range(2):
                e.tensor_add(hsv[:, TT * n: TT * n + TT], hsv[:, TT * n: TT * n + TT],
                             d8r[:, TT * n: TT * n + TT])
            e.drain()
            return e.tensor_mul(qcar[:], qcar[:], lam256[:])
        P.ev("dv", hscorr, reads=[hsn, "lpl", "lam256"], writes=[hsn, "qcar", "d8r"])

        P.ev("dv", lambda e, hbv=hbv, hsv=hsv: e.tensor_copy(hbv[:], hsv[:]),
             reads=[hsn], writes=[hbn])

    def plan_C(t):
        p, x, xsn, xnv, xnn, hsv, hsn, hbv, hbn, ruv, run, rpv, rpn = mvars(t)
        for m in range(NCH):
            bank = PBN[2 + m % 2]
            def y_fn(e, m=m, bank=2 + m % 2, hbv=hbv):
                for k in range(2):
                    op = e.matmul(pb[bank][:, 0:TT], cw_sb[:, D * k + 128 * m: D * k + 128 * m + 128],
                                  hbv[:, C(k)], start=(k == 0), stop=(k == 1))
                return op
            P.ev("pe", y_fn, reads=[hbn, "cw_sb"], writes=[bank])

            def a_fn(e, m=m, bank=2 + m % 2, av=a_sb2[p]):
                return e.tensor_add(av[:, C(m)], av[:, C(m)], pb[bank][:, 0:TT])
            P.ev("dv", a_fn, reads=[bank, f"a_sb{p}"], writes=[f"a_sb{p}"])
            P.ev("ac", lambda e, m=m, av=a_sb2[p]: e.activation(sqbf1[:, C(m)], av[:, C(m)], AF.Square),
                 reads=[f"a_sb{p}"], writes=["sqbf"])

    def plan_D_wk(t):
        p, x, xsn, xnv, xnn, hsv, hsn, hbv, hbn, ruv, run, rpv, rpn = mvars(t)
        wkrot = [5, 2, 3]
        for m in range(NCH):
            bank = PBN[wkrot[m % 3]]
            def wk_fn(e, m=m, bank=wkrot[m % 3]):
                for k in range(NCH):
                    op = e.matmul(pb[bank][:, 0:TT], wk_sb[:, D * k + 128 * m: D * k + 128 * m + 128],
                                  xkbf[:, C(k)], start=(k == 0), stop=(k == NCH - 1))
                return op
            P.ev("pe", wk_fn, reads=["xkbf", "wk_sb"], writes=[bank])

            def kact_fn(e, m=m, bank=wkrot[m % 3]):
                e.activation(krsc[:], pb[bank][:, 0:TT], AF.Relu)
                e.drain()
                return e.activation(kact[:, C(m)], krsc[:], AF.Square)
            P.ev("ac", kact_fn, reads=[bank], writes=["kact", "krsc"])

    def plan_D_rv(t):
        p, x, xsn, xnv, xnn, hsv, hsn, hbv, hbn, ruv, run, rpv, rpn = mvars(t)
        for m in range(NCH):
            rbank = PBN[4 if m % 2 == 0 else 6]
            vbank2 = PBN[7 if m % 2 == 0 else 0]
            def wr_fn(e, m=m, bank=4 if m % 2 == 0 else 6):
                for k in range(NCH):
                    op = e.matmul(pb[bank][:, 0:TT], wr_sb[:, D * k + 128 * m: D * k + 128 * m + 128],
                                  xrbf[:, C(k)], start=(k == 0), stop=(k == NCH - 1))
                return op
            P.ev("pe", wr_fn, reads=["xrbf", "wr_sb"], writes=[rbank])
            P.ev("ac", lambda e, m=m, bank=4 if m % 2 == 0 else 6: e.activation(
                     r_sb[:, TT * (m % 2): TT * (m % 2) + TT], pb[bank][:, 0:TT], AF.Sigmoid),
                 reads=[rbank], writes=[f"rr{m % 2}"])

            def wv_fn(e, m=m, bank=7 if m % 2 == 0 else 0):
                for k in range(NCH):
                    op = e.matmul(pb[bank][:, 0:TT], wv_sb[:, D * k + 128 * m: D * k + 128 * m + 128],
                                  kact[:, C(k)], start=(k == 0), stop=(k == NCH - 1))
                return op
            P.ev("pe", wv_fn, reads=["kact", "wv_sb"], writes=[vbank2])
            P.ev("dv", lambda e, m=m, bank=7 if m % 2 == 0 else 0: e.tensor_mul(
                     fring[:, TT * (m % 2): TT * (m % 2) + TT],
                     r_sb[:, TT * (m % 2): TT * (m % 2) + TT], pb[bank][:, 0:TT]),
                 reads=[vbank2, f"rr{m % 2}"], writes=[f"fr{m % 2}"])
            P.ev("pl", lambda e, m=m, x=x: e.tensor_add(
                     x[:, R(m)], x[:, R(m)], fring[:, TT * (m % 2): TT * (m % 2) + TT]),
                 reads=[f"fr{m % 2}"], writes=[xsn])

    def plan_E_head(t):
        p, x, xsn, xnv, xnn, hsv, hsn, hbv, hbn, ruv, run, rpv, rpn = mvars(t)
        r0, r1, r2, r3 = rr[0]
        def s2a_fn(e):
            for c in range(NCH):
                op = e.matmul(pb[5][0:1, 0:TT], ones1b[:], sqbf1[:, C(c)],
                              start=(c == 0), stop=(c == NCH - 1))
            return op
        P.ev("pe", s2a_fn, reads=["sqbf", "ones1b"], writes=["pb5"])
        P.ev("dv", lambda e: e.tensor_scalar(r0[0:1, 0:TT], pb[5][0:1, 0:TT], 1.0 / D, None, ALU.mult),
             reads=["pb5"], writes=["r0p0"])

        P.ev("ac", lambda e: e.activation(r2[0:1, 0:TT], r0[0:1, 0:TT], AF.Sqrt, bias=cvec[0:1, 1:2]),
             reads=["r0p0", "cvec"], writes=["r2p0"])
        P.ev("dv", lambda e: e.reciprocal(r0[0:1, 0:TT], r2[0:1, 0:TT]),
             reads=["r2p0"], writes=["r0p0"])

        P.ev("pe", lambda e: e.matmul(pb[5][:, 0:TT], onesk1[:], r0[0:1, 0:TT], start=True, stop=True),
             reads=["r0p0", "onesk1"], writes=["pb5"])
        P.ev("dv", lambda e: e.tensor_copy(rzbc[:], pb[5][:, 0:TT]), reads=["pb5"], writes=["rzbc"])

        def z_fn(e, av=a_sb2[t % 2]):
            for c in range(NCH):
                op = e.scalar_tensor_tensor(zbf[:, C(c)], av[:, C(c)], wffn_v[:, c:c + 1],
                                            rzbc[:], ALU.mult, ALU.mult)
            return op
        P.ev("dv", z_fn, reads=[f"a_sb{t % 2}", "rzbc", "wffn_v"], writes=["zbf"])

    ffn_banks = [0, 1, 6, 7]

    def plan_E_mms(t):
        NG = len(FGRP)
        w2q = []

        def plan_w2(m, fw):
            def w2_fn(e, fw=fw, m=m):
                s = m % 3
                for dm in range(NCH):
                    bk = ffn_banks[dm // 2]
                    col = (dm % 2) * TT
                    op = e.matmul(pb[bk][:, col:col + TT],
                                  w2r[0:fw, D * s + 128 * dm: D * s + 128 * dm + 128],
                                  gvbf[0:fw, TT * (m % 3): TT * (m % 3) + TT],
                                  start=(m == 0 and dm % 2 == 0), stop=(m == len(FCH) - 1),
                                  skip_group_check=True)
                return op
            P.ev("pe", w2_fn, reads=[f"gv{m % 3}", f"w2r{m % 3}"],
                 writes=[PBN[b] for b in ffn_banks])
            if m + 3 < len(FCH):
                P.ev("ac", stream_w2(m + 3), reads=["w2s"], writes=[f"w2r{(m + 3) % 3}"], dma=1)

        for gi, (f0g, fwg, ms) in enumerate(FGRP):
            gb = (NG * t + gi) % 2
            for m in ms:
                f0, fw = FCH[m]
                off = f0 - f0g
                gbank = 2 + m % 2
                vbank = 4 + m % 2
                gband = PBN[gbank]
                vband = PBN[vbank]

                def gate_fn(e, fw=fw, off=off, gb=gb, bank=gbank, fwg=fwg):
                    for k in range(NCH):
                        op = e.matmul(pb[bank][0:fw, 0:TT],
                                      w1g[gb][:, fwg * k + off: fwg * k + off + fw],
                                      zbf[:, C(k)], start=(k == 0), stop=(k == NCH - 1))
                    return op
                P.ev("pe", gate_fn, reads=["zbf", f"w1g{gb}"], writes=[gband])

                def val_fn(e, fw=fw, off=off, gb=gb, bank=vbank, fwg=fwg):
                    for k in range(NCH):
                        op = e.matmul(pb[bank][0:fw, 0:TT],
                                      w3g[gb][:, fwg * k + off: fwg * k + off + fw],
                                      zbf[:, C(k)], start=(k == 0), stop=(k == NCH - 1))
                    return op
                P.ev("pe", val_fn, reads=["zbf", f"w3g{gb}"], writes=[vband])

                def sv_evac(e, fw=fw, m=m, gbank=gbank, vbank=vbank):
                    e.activation(gssc[0:fw, TT * (m % 2): TT * (m % 2) + TT],
                                 pb[gbank][0:fw, 0:TT], AF.Silu)
                    return e.activation(vssc[0:fw, TT * (m % 2): TT * (m % 2) + TT],
                                        pb[vbank][0:fw, 0:TT], AF.Identity)
                P.ev("ac", sv_evac, reads=[gband, vband], writes=[f"gs{m % 2}", f"vs{m % 2}"])
                P.ev("pl", lambda e, fw=fw, m=m: e.tensor_mul(
                         gvbf[0:fw, TT * (m % 3): TT * (m % 3) + TT],
                         gssc[0:fw, TT * (m % 2): TT * (m % 2) + TT],
                         vssc[0:fw, TT * (m % 2): TT * (m % 2) + TT]),
                     reads=[f"gs{m % 2}", f"vs{m % 2}"], writes=[f"gv{m % 3}"])

                w2q.append((m, fw))
                if len(w2q) > 2:
                    plan_w2(*w2q.pop(0))
            nidx = NG * t + gi + 2
            if nidx < NG * NT:
                nb = nidx % 2
                P.ev("sy", stream_w13(nb, nidx % NG), reads=["w1s", "w3s"],
                     writes=[f"w1g{nb}", f"w3g{nb}"], dma=2)
        while w2q:
            plan_w2(*w2q.pop(0))
        if t + 1 < NT:
            for m in range(3):
                P.ev("ac", stream_w2(m), reads=["w2s"], writes=[f"w2r{m % 3}"], dma=1)

    def plan_F(t):
        p, x, xsn, xnv, xnn, hsv, hsn, hbv, hbn, ruv, run, rpv, rpn = mvars(t)
        def fin_a(e, x=x, av=a_sb2[p]):
            for m in range(NCH):
                op = e.tensor_add(x[:, R(m)], x[:, R(m)], av[:, C(m)])
            return op
        P.ev("pl", fin_a, reads=[f"a_sb{p}"], writes=[xsn])

        def fin_fn(e, x=x):
            for m in range(NCH):
                bk = ffn_banks[m // 2]
                col = (m % 2) * TT
                op = e.tensor_add(x[:, R(m)], x[:, R(m)], pb[bk][:, col:col + TT])
            return op
        P.ev("dv", fin_fn, reads=[PBN[b] for b in ffn_banks], writes=[xsn])

        def store_fn(e, ic, t=t, x=x):
            return ic(e.dma_start(
                out=outT[:, TT * t: TT * t + TT].rearrange("(c p) w -> p c w", p=128),
                in_=x[:].rearrange("p (c w) -> p c w", w=ET)[:, :, 1:ET]))
        P.ev("pl", store_fn, reads=[xsn], writes=[f"st{t}"], dma=1)

        if t + 2 < NT:
            P.ev("sy", xload(t + 2), writes=[f"xs{t % 2}"], dma=1)

    plan_B(0)
    for t in range(NT):
        plan_C(t)
        plan_D_wk(t)
        plan_E_head(t)
        plan_D_rv(t)
        if t + 1 < NT:
            plan_B(t + 1)
        plan_E_mms(t)
        plan_F(t)

    import contextlib
    es = contextlib.ExitStack()
    sems = {k: es.enter_context(nc.semaphore(f"sem_{k}"))
            for k in list(P.cnt.keys())}
    block = es.enter_context(nc.Block())

    def replay(engine_key, e):
        waited = {}
        for waits, fn, dma, post, own in P.events[engine_key]:
            for sk, v in waits:
                if waited.get(sk, -1) < v:
                    e.wait_ge(sems[sk], v)
                    waited[sk] = v
            if dma:
                amt = 1 if dma == "coll" else 16
                ic = lambda inst, s=own, a=amt: inst.then_inc(sems[s], a)
                fn(e, ic)
            else:
                last = fn(e)
                if engine_key == "pe":
                    last.then_inc(sems["pe"], 1)
                else:
                    e.drain().then_inc(sems[engine_key], 1)
        if engine_key == "ac":
            for i in range(P.NQA):
                if P.cnt[f"qa{i}"]:
                    e.wait_ge(sems[f"qa{i}"], P.cnt[f"qa{i}"])
        if engine_key == "pl":
            for i in range(P.NQG):
                if P.cnt[f"qg{i}"]:
                    e.wait_ge(sems[f"qg{i}"], P.cnt[f"qg{i}"])
            if P.cnt["qc"]:
                e.wait_ge(sems["qc"], P.cnt["qc"])
        if engine_key == "sy":
            for i in range(P.NQS):
                if P.cnt[f"qs{i}"]:
                    e.wait_ge(sems[f"qs{i}"], P.cnt[f"qs{i}"])

    with es:
        @block.tensor
        def _(tensor):
            replay("pe", tensor)

        @block.vector
        def _(vector):
            replay("dv", vector)

        @block.scalar
        def _(scalar):
            replay("ac", scalar)

        @block.gpsimd
        def _(gpsimd):
            replay("pl", gpsimd)

        @block.sync
        def _(sync):
            replay("sy", sync)

    return nc


def kernel(L0_attn_x, L0_attn_log_lambda, L0_attn_B_w, L0_attn_C_w, L0_attn_D_skip,
           L0_attn_ssm_norm_w, L0_attn_ffn_norm_w, L0_attn_w1, L0_attn_w2, L0_attn_w3,
           L0_ffn_W_r, L0_ffn_W_k, L0_ffn_W_v, L0_ffn_mu_r, L0_ffn_mu_k):
    x = np.asarray(L0_attn_x, dtype=np.float32)
    B, T, _ = x.shape
    f32 = lambda a: np.ascontiguousarray(np.asarray(a, dtype=np.float32))
    tr = lambda a: np.ascontiguousarray(np.asarray(a, dtype=np.float32).T)
    v8 = lambda a: np.ascontiguousarray(np.asarray(a, dtype=np.float32).reshape(NCH, 128).T)

    shared = dict(
        wrT=tr(L0_ffn_W_r), wkT=tr(L0_ffn_W_k), wvT=tr(L0_ffn_W_v),
        bwT=tr(L0_attn_B_w), cwT=tr(L0_attn_C_w),
        w1T=tr(L0_attn_w1), w3T=tr(L0_attn_w3), w2T=tr(L0_attn_w2),
        mu_r8=v8(L0_ffn_mu_r), mu_k8=v8(L0_ffn_mu_k), dsk8=v8(L0_attn_D_skip),
        wssm8=v8(L0_attn_ssm_norm_w), wffn8=v8(L0_attn_ffn_norm_w),
        loglam2=np.ascontiguousarray(np.asarray(L0_attn_log_lambda, np.float32).reshape(2, 128).T),
        tI128=np.ascontiguousarray(np.broadcast_to(np.arange(1, TT + 1, dtype=np.float32), (128, TT))),
    )

    in_maps = []
    for core in range(8):
        b, ch = core // 4, core % 4
        s = ch * TC
        if s == 0:
            xe = np.concatenate([np.zeros((1, D), np.float32), x[b, 0:TC, :]], axis=0)
        else:
            xe = x[b, s - 1: s + TC, :]
        m = dict(shared)
        m["xT"] = np.ascontiguousarray(xe.T)
        cexpo = np.empty((128, 8), np.float32)
        csel = np.zeros((128, 8), np.float32)
        for j in range(4):
            for c in range(2):
                if j < ch:
                    cexpo[:, 2 * j + c] = 4096.0 * (ch - 1 - j)
                    csel[:, 2 * j + c] = 1.0
                else:
                    cexpo[:, 2 * j + c] = 1e9
        m["cexpo8"] = cexpo
        m["csel8"] = csel
        in_maps.append(m)

    nc = build_program()
    res = run_bass_kernel_spmd(nc, in_maps, core_ids=list(range(8)))
    out = np.empty((B, T, D), np.float32)
    for core in range(8):
        b, ch = core // 4, core % 4
        out[b, ch * TC:(ch + 1) * TC, :] = res.results[core]["outT"].T
    return out


# version 29
# speedup vs baseline: 1.0120x; 1.0120x over previous
import numpy as np
import concourse.bass as bass
import concourse.mybir as mybir
from concourse.bass_utils import run_bass_kernel_spmd

F32 = mybir.dt.float32
BF16 = mybir.dt.bfloat16
AF = mybir.ActivationFunctionType
ALU = mybir.AluOpType

D = 1024
NCH = 8
NS = 256
DFF = 2736
TT = 256
ET = TT + 1
TC = 4096
NT = TC // TT
EPS_LN = 1e-5
EPS_RMS = 1e-6

FCH = [(m * 128, min(128, DFF - m * 128)) for m in range((DFF + 127) // 128)]
FGRP = []
_m = 0
while _m < len(FCH):
    g = FCH[_m:_m + 2]
    FGRP.append((g[0][0], sum(w for _, w in g), list(range(_m, _m + len(g)))))
    _m += 2


class Plan:

    NQS = 48
    NQG = 24
    NQA = 24

    def __init__(self):
        self.events = {k: [] for k in ("pe", "dv", "ac", "pl", "sy")}
        self.cnt = {"pe": 0, "dv": 0, "ac": 0, "pl": 0, "qc": 0}
        for i in range(self.NQS):
            self.cnt[f"qs{i}"] = 0
        for i in range(self.NQG):
            self.cnt[f"qg{i}"] = 0
        for i in range(self.NQA):
            self.cnt[f"qa{i}"] = 0
        self.qs_rr = 0
        self.qg_rr = 0
        self.qa_rr = 0
        self.last_w = {}
        self.last_r = {}

    def ev(self, eng, fn, reads=(), writes=(), extra=(), dma=0):
        waits = {}
        def add(sk, v):
            if v > waits.get(sk, -1):
                waits[sk] = v
        if dma == "coll":
            own = "qc"
        elif dma:
            if eng == "sy":
                own = f"qs{self.qs_rr % self.NQS}"
                self.qs_rr += 1
            elif eng == "ac":
                own = f"qa{self.qa_rr % self.NQA}"
                self.qa_rr += 1
            else:
                own = f"qg{self.qg_rr % self.NQG}"
                self.qg_rr += 1
        else:
            own = eng
        if dma:
            if self.cnt[own]:
                add(own, self.cnt[own])
        for b in reads:
            if b in self.last_w:
                sk, v = self.last_w[b]
                if sk != eng or dma:
                    if sk != own:
                        add(sk, v)
        for b in writes:
            if b in self.last_w:
                sk, v = self.last_w[b]
                if sk != own:
                    add(sk, v)
            for sk, v in self.last_r.get(b, {}).items():
                if sk != own:
                    add(sk, v)
        for sk, v in extra:
            add(sk, v)
        for sk, v in waits.items():
            assert v <= self.cnt[sk], (sk, v, self.cnt[sk])
        if dma == "coll":
            self.cnt[own] += 1
        elif dma:
            self.cnt[own] += 16 * dma
        else:
            self.cnt[own] += 1
        post = self.cnt[own]
        self.events[eng].append((sorted(waits.items()), fn, dma, post, own))
        for b in writes:
            self.last_w[b] = (own, post)
            self.last_r.pop(b, None)
        for b in reads:
            self.last_r.setdefault(b, {})[own] = post
        return post


def build_program():
    nc = bass.Bass("TRN2", target_bir_lowering=False)

    xT = nc.dram_tensor("xT", [D, TC + 1], F32, kind="ExternalInput")
    wrT = nc.dram_tensor("wrT", [D, D], F32, kind="ExternalInput")
    wkT = nc.dram_tensor("wkT", [D, D], F32, kind="ExternalInput")
    wvT = nc.dram_tensor("wvT", [D, D], F32, kind="ExternalInput")
    bwT = nc.dram_tensor("bwT", [D, NS], F32, kind="ExternalInput")
    cwT = nc.dram_tensor("cwT", [NS, D], F32, kind="ExternalInput")
    w1T = nc.dram_tensor("w1T", [D, DFF], F32, kind="ExternalInput")
    w3T = nc.dram_tensor("w3T", [D, DFF], F32, kind="ExternalInput")
    w2T = nc.dram_tensor("w2T", [DFF, D], F32, kind="ExternalInput")
    mu_r8 = nc.dram_tensor("mu_r8", [128, NCH], F32, kind="ExternalInput")
    mu_k8 = nc.dram_tensor("mu_k8", [128, NCH], F32, kind="ExternalInput")
    dsk8 = nc.dram_tensor("dsk8", [128, NCH], F32, kind="ExternalInput")
    wssm8 = nc.dram_tensor("wssm8", [128, NCH], F32, kind="ExternalInput")
    wffn8 = nc.dram_tensor("wffn8", [128, NCH], F32, kind="ExternalInput")
    loglam2 = nc.dram_tensor("loglam2", [128, 2], F32, kind="ExternalInput")
    cexpo8 = nc.dram_tensor("cexpo8", [128, 8], F32, kind="ExternalInput")
    csel8 = nc.dram_tensor("csel8", [128, 8], F32, kind="ExternalInput")
    tI128 = nc.dram_tensor("tI128", [128, TT], F32, kind="ExternalInput")
    outT = nc.dram_tensor("outT", [D, TC], F32, kind="ExternalOutput")
    w1s = nc.dram_tensor("w1s", [D, DFF], BF16)
    w3s = nc.dram_tensor("w3s", [D, DFF], BF16)
    w2s = nc.dram_tensor("w2s", [DFF, D], BF16)
    hss = nc.dram_tensor("hss", [NS, TC], F32)
    rowS = nc.dram_tensor("rowS", [NT, 3 * ET], F32)
    hloc = nc.dram_tensor("hloc", [128, 2], F32)
    hgath = nc.dram_tensor("hgath", [512, 2], F32)

    sb = nc.alloc_sbuf_tensor
    xs = [sb("xs0", [128, NCH * ET], F32), sb("xs1", [128, NCH * ET], F32)]
    xbfp = [sb("xbfp0", [128, NCH * ET], BF16), sb("xbfp1", [128, NCH * ET], BF16)]
    sqbf1 = sb("sqbf", [128, NCH * ET], BF16)
    sqbfp = [sqbf1, sqbf1]
    xn = [sb("xn0", [128, NCH * ET], F32), sb("xn1", [128, NCH * ET], F32)]
    hs = [sb("hs0", [128, 2 * TT], F32), sb("hs1", [128, 2 * TT], F32)]
    h0b = sb("h0b", [128, 2], F32)
    hsbf = [sb("hsbf0", [128, 2 * TT], BF16), sb("hsbf1", [128, 2 * TT], BF16)]
    a_sb2 = [sb("a_sb0", [128, NCH * TT], F32), sb("a_sb1", [128, NCH * TT], F32)]
    d8r = sb("d8r", [128, 2 * TT], F32)
    xrbf = sb("xrbf", [128, NCH * TT], BF16)
    xkbf = sb("xkbf", [128, NCH * TT], BF16)
    r_sb = sb("r_sb", [128, 2 * TT], BF16)
    kact = sb("kact", [128, NCH * TT], BF16)
    fring = sb("fring", [128, 2 * TT], F32)
    zbf = sb("zbf", [128, NCH * TT], BF16)
    krsc = sb("krsc", [128, TT], F32)
    gssc = sb("gssc", [128, 2 * TT], BF16)
    vssc = sb("vssc", [128, 2 * TT], BF16)
    gvbf = sb("gvbf", [128, 3 * TT], BF16)
    rubc = [sb("rubc0", [128, TT], F32), sb("rubc1", [128, TT], F32)]
    rzbc = sb("rzbc", [128, TT], F32)
    mbc_sb = sb("mbc_sb", [128, ET], F32)
    rsbc_sb = sb("rsbc_sb", [128, ET], F32)
    rsc = sb("rsc", [1, ET], F32)
    rr = [[sb(f"r{i}p{p}", [1, ET], F32) for i in range(4)] for p in range(2)]
    wr_sb = sb("wr_sb", [128, NCH * D], BF16)
    wk_sb = sb("wk_sb", [128, NCH * D], BF16)
    wv_sb = sb("wv_sb", [128, NCH * D], BF16)
    wb_sb = sb("wb_sb", [128, NCH * NS], BF16)
    cw_sb = sb("cw_sb", [128, 2 * D], BF16)
    w1g = [sb("w1g0", [128, 2048], BF16), sb("w1g1", [128, 2048], BF16)]
    w3g = [sb("w3g0", [128, 2048], BF16), sb("w3g1", [128, 2048], BF16)]
    w2r = sb("w2r", [128, 3 * D], BF16)
    lam_rep = sb("lam_rep", [128, 2 * TT], F32)
    ones1b = sb("ones1b", [128, 1], BF16)
    onesk1 = sb("onesk1", [1, 128], F32)
    mu_r_v = sb("mu_r_v", [128, NCH], F32)
    mu_k_v = sb("mu_k_v", [128, NCH], F32)
    dsk_v = sb("dsk_v", [128, NCH], F32)
    wssm_v = sb("wssm_v", [128, NCH], F32)
    wffn_v = sb("wffn_v", [128, NCH], F32)
    loglam_v = sb("loglam_v", [128, 2], F32)
    lam_v = sb("lam_v", [128, 2], F32)
    cvec = sb("cvec", [128, 2], F32)
    bu_sb1 = sb("bu_sb", [128, 2 * TT], F32)
    bu_sb = [bu_sb1, bu_sb1]
    negc = sb("negc", [1, NS], BF16)
    mbf1 = sb("mbf", [1, TT], BF16)
    mbf = [mbf1, mbf1]
    rowpack = [sb("rowpack0", [1, 3 * ET], F32), sb("rowpack1", [1, 3 * ET], F32)]
    heg = sb("heg", [128, 8], F32)
    lnl = sb("lnl", [128, 2], F32)
    lnl8 = sb("lnl8", [128, 8], F32)
    coef = sb("coef", [128, 8], F32)
    prodc = sb("prodc", [128, 8], F32)
    qcar = sb("qcar", [128, 2], F32)
    h0add = sb("h0add", [128, 2], F32)
    lam256 = sb("lam256", [128, 2], F32)
    lpl = sb("lpl", [128, 2 * TT], F32)
    hend_sb = sb("hend_sb", [128, 2], F32)
    cexpo_v = sb("cexpo_v", [128, 8], F32)
    csel_v = sb("csel_v", [128, 8], F32)
    tI_v = sb("tI_v", [128, TT], F32)

    pb = [nc.alloc_psum_tensor(f"pb{i}", [128, 512], F32) for i in range(8)]
    PBN = [f"pb{i}" for i in range(8)]

    P = Plan()
    E = lambda c: slice(ET * c, ET * c + ET)
    R = lambda c: slice(ET * c + 1, ET * c + ET)
    C = lambda c: slice(TT * c, TT * c + TT)

    def cast_dma(dst, src, rr):
        def fn(e, ic):
            if rr:
                return ic(e.dma_start(out=dst[:].rearrange("p (k m) -> p k m", k=rr),
                                      in_=src[:, :].rearrange("(k p) m -> p k m", p=128)))
            return ic(e.dma_start(out=dst[:, :], in_=src[:, :]))
        return fn
    P.ev("pl", cast_dma(cw_sb, cwT, 2), writes=["cw_sb"], dma=1)
    P.ev("sy", lambda e, ic: ic(e.dma_start(
             out=a_sb2[0][:, 0:NCH * NS].rearrange("p (k n) -> p k n", n=NS),
             in_=bwT[:, :].rearrange("(k p) n -> p k n", p=128))),
         writes=["a_sb0"], dma=1)

    def vec_loads(e, ic):
        ic(e.dma_start(out=mu_r_v[:], in_=mu_r8[:, :]))
        ic(e.dma_start(out=mu_k_v[:], in_=mu_k8[:, :]))
        ic(e.dma_start(out=dsk_v[:], in_=dsk8[:, :]))
        ic(e.dma_start(out=wssm_v[:], in_=wssm8[:, :]))
        ic(e.dma_start(out=wffn_v[:], in_=wffn8[:, :]))
        ic(e.dma_start(out=cexpo_v[:], in_=cexpo8[:, :]))
        ic(e.dma_start(out=csel_v[:], in_=csel8[:, :]))
        ic(e.dma_start(out=tI_v[:], in_=tI128[:, :]))
        return ic(e.dma_start(out=loglam_v[:], in_=loglam2[:, :]))
    P.ev("sy", vec_loads, writes=["mu_r_v", "mu_k_v", "dsk_v", "wssm_v", "wffn_v",
                                  "loglam_v", "cexpo_v", "csel_v", "tI_v"], dma=9)

    def consts(e):
        e.memset(ones1b[:], 1.0)
        e.memset(onesk1[:], 1.0)
        e.memset(h0b[:], 0.0)
        e.memset(cvec[:, 0:1], EPS_LN)
        e.memset(cvec[:, 1:2], EPS_RMS)
        return e.memset(lam_rep[:], 1.0)
    P.ev("dv", consts, writes=["ones1b", "onesk1", "h0b", "lam_rep", "cvec"])

    P.ev("ac", lambda e: e.activation(lam_v[:], loglam_v[:], AF.Sigmoid),
         reads=["loglam_v"], writes=["lam_v"])

    def lamrep(e):
        e.tensor_scalar(lam_rep[:, 0:TT], lam_rep[:, 0:TT], lam_v[:, 0:1], None, ALU.mult)
        return e.tensor_scalar(lam_rep[:, TT:2 * TT], lam_rep[:, TT:2 * TT], lam_v[:, 1:2], None, ALU.mult)
    P.ev("dv", lamrep, reads=["lam_v"], writes=["lam_rep"])

    def wbfold(e):
        for k in range(NCH):
            op = e.tensor_scalar(wb_sb[:, NS * k: NS * k + NS], a_sb2[0][:, NS * k: NS * k + NS],
                                 wssm_v[:, k:k + 1], None, ALU.mult)
        return op
    P.ev("dv", wbfold, reads=["a_sb0", "wssm_v"], writes=["wb_sb"])

    P.ev("ac", lambda e: e.activation(lnl[:], lam_v[:], AF.Ln), reads=["lam_v"], writes=["lnl"])

    def lnl8_fn(e):
        for j in range(4):
            op = e.tensor_copy(lnl8[:, 2 * j: 2 * j + 2], lnl[:])
        return op
    P.ev("dv", lnl8_fn, reads=["lnl"], writes=["lnl8"])

    def lpl_pre(e):
        e.tensor_scalar(lpl[:, 0:TT], tI_v[:], lnl[:, 0:1], None, ALU.mult)
        e.tensor_scalar(lpl[:, TT:2 * TT], tI_v[:], lnl[:, 1:2], None, ALU.mult)
        return e.tensor_scalar(lam256[:], lnl[:], float(TT), None, ALU.mult)
    P.ev("dv", lpl_pre, reads=["lnl", "tI_v"], writes=["lpl", "lam256"])

    def lpl_exp(e):
        e.activation(lpl[:], lpl[:], AF.Exp)
        e.drain()
        return e.activation(lam256[:], lam256[:], AF.Exp)
    P.ev("ac", lpl_exp, reads=[], writes=["lpl", "lam256"])

    def colsum_fn(e):
        for k in range(NCH):
            op = e.matmul(pb[0][0:1, 0:NS], ones1b[:], wb_sb[:, NS * k: NS * k + NS],
                          start=(k == 0), stop=(k == NCH - 1))
        return op
    P.ev("pe", colsum_fn, reads=["wb_sb", "ones1b"], writes=["pb0"])
    P.ev("dv", lambda e: e.tensor_scalar(negc[:], pb[0][0:1, 0:NS], -1.0, None, ALU.mult),
         reads=["pb0"], writes=["negc"])

    def xload(t):
        x = xs[t % 2]
        def fn(e, ic):
            return ic(e.dma_start(
                out=x[:].rearrange("p (c w) -> p c w", w=ET),
                in_=xT[:, TT * t: TT * t + ET].rearrange("(c p) w -> p c w", p=128)))
        return fn

    def stream_w13(b, g):
        f0, fw, _ = FGRP[g]
        def fn(e, ic):
            ic(e.dma_start(out=w1g[b][:, 0:NCH * fw].rearrange("p (k f) -> p k f", f=fw),
                           in_=w1s[:, f0:f0 + fw].rearrange("(k p) f -> p k f", p=128)))
            return ic(e.dma_start(out=w3g[b][:, 0:NCH * fw].rearrange("p (k f) -> p k f", f=fw),
                                  in_=w3s[:, f0:f0 + fw].rearrange("(k p) f -> p k f", p=128)))
        return fn

    def stream_w2(m):
        f0, fw = FCH[m]
        s = m % 3
        def fn(e, ic):
            return ic(e.dma_start(out=w2r[0:fw, D * s:D * s + D], in_=w2s[f0:f0 + fw, :]))
        return fn

    P.ev("sy", xload(0), writes=["xs0"], dma=1)
    P.ev("sy", xload(1), writes=["xs1"], dma=1)

    def s_mms(src, bank):
        def fn(e):
            for c in range(NCH):
                e.matmul(bank[0:1, 0:1], ones1b[:], src[:, ET * c: ET * c + 1],
                         start=(c == 0), stop=(c == NCH - 1))
            for c in range(NCH):
                op = e.matmul(bank[0:1, 1:ET], ones1b[:], src[:, R(c)],
                              start=(c == 0), stop=(c == NCH - 1))
            return op
        return fn

    def p0_vars(t):
        p = t % 2
        return dict(p=p, x=xs[p], xsn=f"xs{p}", rws=rr[p],
                    rn=[f"r{i}p{p}" for i in range(4)],
                    sA=0 if p == 0 else 2, sB=1 if p == 0 else 3,
                    gB=4 if p == 0 else 7, bu=bu_sb[p], bun="bu_sb",
                    ruv=rubc[p], run=f"rubc{p}", hsv=hs[p], hsn=f"hs{p}")

    def plan_p0_load(t):
        v = p0_vars(t)
        p, x, xsn = v["p"], v["x"], v["xsn"]
        sA, sB = v["sA"], v["sB"]
        xb = xbfp[p]
        xbn = f"xbfp{p}"
        sq = sqbf1
        sqn = "sqbf"
        P.ev("dv", lambda e, x=x, xb=xb: e.tensor_copy(xb[:], x[:]), reads=[xsn], writes=[xbn])
        P.ev("dv", lambda e, x=x, sq=sq: e.tensor_mul(sq[:], x[:], x[:]), reads=[xsn], writes=["sqbf"])
        if t + 2 < NT:
            P.ev("sy", xload(t + 2), writes=[xsn], dma=1)
        P.ev("pe", s_mms(xb, pb[sA]), reads=[xbn, "ones1b"], writes=[PBN[sA]])
        P.ev("pe", s_mms(sq, pb[sB]), reads=["sqbf", "ones1b"], writes=[PBN[sB]])

    def plan_p0_rest(t):
        v = p0_vars(t)
        p, x, xsn = v["p"], v["x"], v["xsn"]
        r0, r1, r2, r3 = v["rws"]
        r0n, r1n, r2n, r3n = v["rn"]
        sA, sB, gB = v["sA"], v["sB"], v["gB"]
        bu, bun, ruv, run, hsv, hsn = v["bu"], v["bun"], v["ruv"], v["run"], v["hsv"], v["hsn"]
        xb = xbfp[p]
        xbn = f"xbfp{p}"

        def rows1(e, r0=r0, r1=r1, r2=r2, sA=sA, sB=sB):
            e.tensor_scalar(r0[:], pb[sA][0:1, 0:ET], 1.0 / D, None, ALU.mult)
            e.tensor_scalar(r1[:], pb[sB][0:1, 0:ET], 1.0 / D, None, ALU.mult)
            e.drain()
            e.tensor_mul(r2[:], r0[:], r0[:])
            e.drain()
            return e.tensor_sub(r1[:], r1[:], r2[:])
        P.ev("dv", rows1, reads=[PBN[sA], PBN[sB]], writes=[r0n, r1n, r2n])

        P.ev("ac", lambda e, r1=r1, r2=r2: e.activation(r2[:], r1[:], AF.Sqrt, bias=cvec[0:1, 0:1]),
             reads=[r1n, "cvec"], writes=[r2n])
        def rows2b(e, r1=r1, r2=r2, r3=r3):
            e.reciprocal(r3[:], r2[:])
            e.drain()
            e.tensor_mul(r2[:], r3[:], r3[:])
            e.drain()
            return e.tensor_mul(r1[:], r1[:], r2[:])
        P.ev("dv", rows2b, reads=[r2n], writes=[r1n, r2n, r3n])
        P.ev("ac", lambda e, r1=r1, r2=r2: e.activation(r2[:], r1[:], AF.Sqrt, bias=cvec[0:1, 1:2]),
             reads=[r1n, "cvec"], writes=[r2n])
        P.ev("dv", lambda e, r1=r1, r2=r2: e.reciprocal(r1[:], r2[:]),
             reads=[r2n], writes=[r1n])

        def rows4(e, r0=r0, r1=r1, r2=r2, r3=r3, p=p):
            e.tensor_mul(r2[:], r3[:], r1[:])
            e.tensor_copy(mbf[p][:], r0[0:1, 1:ET])
            e.tensor_copy(rowpack[p][0:1, 0:ET], r0[:])
            e.tensor_copy(rowpack[p][0:1, ET:2 * ET], r3[:])
            e.tensor_copy(rowpack[p][0:1, 2 * ET:2 * ET + TT], r1[0:1, 1:ET])
            return e.tensor_copy(rowpack[p][0:1, 3 * ET - 1:3 * ET], r1[0:1, 0:1])
        P.ev("dv", rows4, reads=[r0n, r1n, r3n], writes=[r2n, "mbf", f"rowpack{p}"])

        P.ev("pe", lambda e, r2=r2, gB=gB: e.matmul(pb[gB][:, 0:TT], onesk1[:], r2[0:1, 1:ET],
                                                    start=True, stop=True),
             reads=[r2n, "onesk1"], writes=[PBN[gB]])
        P.ev("dv", lambda e, ruv=ruv, gB=gB: e.tensor_copy(ruv[:], pb[gB][:, 0:TT]),
             reads=[PBN[gB]], writes=[run])

        for n in range(2):
            def bu_fn(e, n=n, p=p, xb=xb):
                for k in range(NCH):
                    e.matmul(pb[5 + n][:, 0:TT], wb_sb[:, NS * k + 128 * n: NS * k + 128 * n + 128],
                             xb[:, R(k)], start=(k == 0), stop=False, skip_group_check=True)
                return e.matmul(pb[5 + n][:, 0:TT], negc[0:1, 128 * n: 128 * n + 128],
                                mbf[p][:], start=False, stop=True, skip_group_check=True)
            P.ev("pe", bu_fn, reads=[xbn, "wb_sb", "negc", "mbf"], writes=[PBN[5 + n]])
            P.ev("dv", lambda e, n=n, bu=bu, ruv=ruv: e.tensor_mul(
                     bu[:, TT * n: TT * n + TT], pb[5 + n][:, 0:TT], ruv[:]),
                 reads=[PBN[5 + n], run], writes=[bun])

        if t > 0:
            def h0copy(e, hp=hs[1 - p]):
                e.tensor_copy(h0b[:, 0:1], hp[:, TT - 1:TT])
                return e.tensor_copy(h0b[:, 1:2], hp[:, 2 * TT - 1:2 * TT])
            P.ev("dv", h0copy, reads=[f"hs{1 - p}"], writes=["h0b"])

        def scan_fn(e, bu=bu, hsv=hsv):
            e.tensor_tensor_scan(hsv[:, 0:TT], lam_rep[:, 0:TT], bu[:, 0:TT],
                                 h0b[:, 0:1], ALU.mult, ALU.add)
            return e.tensor_tensor_scan(hsv[:, TT:2 * TT], lam_rep[:, TT:2 * TT],
                                        bu[:, TT:2 * TT], h0b[:, 1:2], ALU.mult, ALU.add)
        P.ev("dv", scan_fn, reads=[bun, "lam_rep", "h0b"], writes=[hsn])

        def p0store(e, ic, t=t, hsv=hsv, p=p):
            ic(e.dma_start(out=hss[:, TT * t: TT * t + TT].rearrange("(n p) w -> p n w", p=128),
                           in_=hsv[:].rearrange("p (n w) -> p n w", w=TT)))
            return ic(e.dma_start(out=rowS[t:t + 1, :], in_=rowpack[p][:]))
        P.ev("sy", p0store, reads=[hsn, f"rowpack{p}"], writes=[f"hss{t}", f"rowS{t}"], dma=2)


    plan_p0_load(0)
    plan_p0_load(1)
    for t in range(NT):
        plan_p0_rest(t)
        if t + 2 < NT:
            plan_p0_load(t + 2)
        if t == 1:
            P.ev("pl", cast_dma(wr_sb, wrT, NCH), writes=["wr_sb"], dma=1)
            P.ev("pl", cast_dma(wk_sb, wkT, NCH), writes=["wk_sb"], dma=1)
            P.ev("pl", cast_dma(wv_sb, wvT, NCH), writes=["wv_sb"], dma=1)
            P.ev("pl", cast_dma(w1s, w1T, 0), writes=["w1s"], dma=1)
            P.ev("pl", cast_dma(w3s, w3T, 0), writes=["w3s"], dma=1)
            P.ev("pl", cast_dma(w2s, w2T, 0), writes=["w2s"], dma=1)
            P.ev("sy", stream_w13(0, 0), reads=["w1s", "w3s"], writes=["w1g0", "w3g0"], dma=2)
            P.ev("sy", stream_w13(1, 1), reads=["w1s", "w3s"], writes=["w1g1", "w3g1"], dma=2)
            for mm3 in range(3):
                P.ev("ac", stream_w2(mm3), reads=["w2s"], writes=[f"w2r{mm3 % 3}"], dma=1)

    def hend_fn(e):
        e.tensor_copy(hend_sb[:, 0:1], hs[(NT - 1) % 2][:, TT - 1:TT])
        return e.tensor_copy(hend_sb[:, 1:2], hs[(NT - 1) % 2][:, 2 * TT - 1:2 * TT])
    P.ev("dv", hend_fn, reads=[f"hs{(NT - 1) % 2}"], writes=["hend_sb"])
    P.ev("pl", lambda e, ic: ic(e.dma_start(out=hloc[:, :], in_=hend_sb[:])),
         reads=["hend_sb"], writes=["hloc"], dma=1)
    P.ev("pl", lambda e, ic: ic(e.collective_compute(
             "AllGather", ALU.bypass, replica_groups=[[0, 1, 2, 3], [4, 5, 6, 7]],
             ins=[hloc.ap().opt()], outs=[hgath.ap().opt()])),
         reads=["hloc"], writes=["hgath"], dma="coll")
    P.ev("pl", lambda e, ic: ic(e.dma_start(out=heg[:].rearrange("p (j c) -> p j c", c=2),
                                            in_=hgath[:, :].rearrange("(j p) c -> p j c", p=128))),
         reads=["hgath"], writes=["heg"], dma=1)

    def coef_fn(e):
        e.tensor_mul(coef[:], cexpo_v[:], lnl8[:])
        return e.drain()
    P.ev("dv", coef_fn, reads=["cexpo_v", "lnl8"], writes=["coef"])
    P.ev("ac", lambda e: e.activation(coef[:], coef[:], AF.Exp), reads=[], writes=["coef"])
    P.ev("dv", lambda e: e.tensor_mul(coef[:], coef[:], csel_v[:]), reads=["csel_v"], writes=["coef"])

    def h0add_fn(e):
        e.tensor_mul(prodc[:], coef[:], heg[:])
        e.drain()
        e.tensor_add(h0add[:], prodc[:, 0:2], prodc[:, 2:4])
        e.drain()
        e.tensor_add(h0add[:], h0add[:], prodc[:, 4:6])
        e.drain()
        e.tensor_add(h0add[:], h0add[:], prodc[:, 6:8])
        e.drain()
        return e.tensor_copy(qcar[:], h0add[:])
    P.ev("dv", h0add_fn, reads=["coef", "heg"], writes=["prodc", "h0add", "qcar"])

    P.ev("sy", xload(0), writes=["xs0"], dma=1)
    P.ev("sy", xload(1), writes=["xs1"], dma=1)
    def mvars(t):
        p = t % 2
        return (p, xs[p], f"xs{p}", xn[p], f"xn{p}", hs[p], f"hs{p}",
                hsbf[p], f"hsbf{p}", rubc[p], f"rubc{p}", rowpack[p], f"rowpack{p}")

    def plan_B(t):
        p, x, xsn, xnv, xnn, hsv, hsn, hbv, hbn, ruv, run, rpv, rpn = mvars(t)

        def rowload(e, ic, t=t, rpv=rpv, hsv=hsv):
            ic(e.dma_start(out=rpv[:], in_=rowS[t:t + 1, :]))
            return ic(e.dma_start(out=hsv[:].rearrange("p (n w) -> p n w", w=TT),
                                  in_=hss[:, TT * t: TT * t + TT].rearrange("(n p) w -> p n w", p=128)))
        P.ev("sy", rowload, reads=[f"hss{t}", f"rowS{t}"], writes=[rpn, hsn], dma=2)

        P.ev("pe", lambda e, rpv=rpv: (
                 e.matmul(pb[5][:, 0:1], onesk1[:], rpv[0:1, 0:1], start=True, stop=True),
                 e.matmul(pb[5][:, 1:ET], onesk1[:], rpv[0:1, 1:ET], start=True, stop=True))[-1],
             reads=[rpn, "onesk1"], writes=["pb5"])
        P.ev("dv", lambda e: e.tensor_copy(mbc_sb[:], pb[5][:, 0:ET]), reads=["pb5"], writes=["mbc_sb"])
        P.ev("pe", lambda e, rpv=rpv: (
                 e.matmul(pb[5][:, 0:1], onesk1[:], rpv[0:1, ET:ET + 1], start=True, stop=True),
                 e.matmul(pb[5][:, 1:ET], onesk1[:], rpv[0:1, ET + 1:2 * ET], start=True, stop=True))[-1],
             reads=[rpn, "onesk1"], writes=["pb5"])
        P.ev("dv", lambda e: e.tensor_copy(rsbc_sb[:], pb[5][:, 0:ET]), reads=["pb5"], writes=["rsbc_sb"])
        P.ev("pe", lambda e, rpv=rpv: e.matmul(pb[5][:, 0:TT], onesk1[:],
                                               rpv[0:1, 2 * ET:2 * ET + TT], start=True, stop=True),
             reads=[rpn, "onesk1"], writes=["pb5"])
        P.ev("dv", lambda e, ruv=ruv: e.tensor_copy(ruv[:], pb[5][:, 0:TT]),
             reads=["pb5"], writes=[run])

        def xn_fn(e, x=x, xnv=xnv):
            for c in range(NCH):
                e.tensor_sub(xnv[:, E(c)], x[:, E(c)], mbc_sb[:])
            e.drain()
            for c in range(NCH):
                op = e.tensor_mul(xnv[:, E(c)], xnv[:, E(c)], rsbc_sb[:])
            return op
        P.ev("dv", xn_fn, reads=[xsn, "mbc_sb", "rsbc_sb"], writes=[xnn])

        def abase_fn(e, xnv=xnv, ruv=ruv, av=a_sb2[p]):
            for c in range(NCH):
                sl = d8r[:, TT * (c % 2): TT * (c % 2) + TT]
                e.scalar_tensor_tensor(sl, xnv[:, R(c)], wssm_v[:, c:c + 1],
                                       ruv[:], ALU.mult, ALU.mult)
                e.drain()
                op = e.scalar_tensor_tensor(av[:, C(c)], sl, dsk_v[:, c:c + 1],
                                            xnv[:, R(c)], ALU.mult, ALU.add)
            return op
        P.ev("dv", abase_fn, reads=[xnn, run, "wssm_v", "dsk_v"], writes=[f"a_sb{p}", "d8r"])

        def mix_fn(e, xnv=xnv):
            for c in range(NCH):
                sl = d8r[:, TT * (c % 2): TT * (c % 2) + TT]
                e.tensor_sub(sl, xnv[:, R(c)], xnv[:, ET * c: ET * c + TT])
                e.drain()
                e.scalar_tensor_tensor(xrbf[:, C(c)], sl, mu_r_v[:, c:c + 1],
                                       xnv[:, ET * c: ET * c + TT], ALU.mult, ALU.add)
                op = e.scalar_tensor_tensor(xkbf[:, C(c)], sl, mu_k_v[:, c:c + 1],
                                            xnv[:, ET * c: ET * c + TT], ALU.mult, ALU.add)
            return op
        P.ev("dv", mix_fn, reads=[xnn, "mu_r_v", "mu_k_v"], writes=["xrbf", "xkbf", "d8r"])

        def hscorr(e, hsv=hsv):
            for n in range(2):
                e.tensor_scalar(d8r[:, TT * n: TT * n + TT], lpl[:, TT * n: TT * n + TT],
                                qcar[:, n:n + 1], None, ALU.mult)
            e.drain()
            for n in range(2):
                e.tensor_add(hsv[:, TT * n: TT * n + TT], hsv[:, TT * n: TT * n + TT],
                             d8r[:, TT * n: TT * n + TT])
            e.drain()
            return e.tensor_mul(qcar[:], qcar[:], lam256[:])
        P.ev("dv", hscorr, reads=[hsn, "lpl", "lam256"], writes=[hsn, "qcar", "d8r"])

        P.ev("dv", lambda e, hbv=hbv, hsv=hsv: e.tensor_copy(hbv[:], hsv[:]),
             reads=[hsn], writes=[hbn])

    def plan_C(t):
        p, x, xsn, xnv, xnn, hsv, hsn, hbv, hbn, ruv, run, rpv, rpn = mvars(t)
        for m in range(NCH):
            bank = PBN[2 + m % 2]
            def y_fn(e, m=m, bank=2 + m % 2, hbv=hbv):
                for k in range(2):
                    op = e.matmul(pb[bank][:, 0:TT], cw_sb[:, D * k + 128 * m: D * k + 128 * m + 128],
                                  hbv[:, C(k)], start=(k == 0), stop=(k == 1))
                return op
            P.ev("pe", y_fn, reads=[hbn, "cw_sb"], writes=[bank])

            def a_fn(e, m=m, bank=2 + m % 2, av=a_sb2[p]):
                return e.tensor_add(av[:, C(m)], av[:, C(m)], pb[bank][:, 0:TT])
            P.ev("dv", a_fn, reads=[bank, f"a_sb{p}"], writes=[f"a_sb{p}"])
            P.ev("ac", lambda e, m=m, av=a_sb2[p]: e.activation(sqbf1[:, C(m)], av[:, C(m)], AF.Square),
                 reads=[f"a_sb{p}"], writes=["sqbf"])

    def plan_D_wk(t):
        p, x, xsn, xnv, xnn, hsv, hsn, hbv, hbn, ruv, run, rpv, rpn = mvars(t)
        wkrot = [5, 2, 3]
        for m in range(NCH):
            bank = PBN[wkrot[m % 3]]
            def wk_fn(e, m=m, bank=wkrot[m % 3]):
                for k in range(NCH):
                    op = e.matmul(pb[bank][:, 0:TT], wk_sb[:, D * k + 128 * m: D * k + 128 * m + 128],
                                  xkbf[:, C(k)], start=(k == 0), stop=(k == NCH - 1))
                return op
            P.ev("pe", wk_fn, reads=["xkbf", "wk_sb"], writes=[bank])

            def kact_fn(e, m=m, bank=wkrot[m % 3]):
                e.activation(krsc[:], pb[bank][:, 0:TT], AF.Relu)
                e.drain()
                return e.activation(kact[:, C(m)], krsc[:], AF.Square)
            P.ev("ac", kact_fn, reads=[bank], writes=["kact", "krsc"])

    def plan_D_rv(t):
        p, x, xsn, xnv, xnn, hsv, hsn, hbv, hbn, ruv, run, rpv, rpn = mvars(t)
        for m in range(NCH):
            rbank = PBN[4 if m % 2 == 0 else 6]
            vbank2 = PBN[7 if m % 2 == 0 else 0]
            def wr_fn(e, m=m, bank=4 if m % 2 == 0 else 6):
                for k in range(NCH):
                    op = e.matmul(pb[bank][:, 0:TT], wr_sb[:, D * k + 128 * m: D * k + 128 * m + 128],
                                  xrbf[:, C(k)], start=(k == 0), stop=(k == NCH - 1))
                return op
            P.ev("pe", wr_fn, reads=["xrbf", "wr_sb"], writes=[rbank])
            P.ev("ac", lambda e, m=m, bank=4 if m % 2 == 0 else 6: e.activation(
                     r_sb[:, TT * (m % 2): TT * (m % 2) + TT], pb[bank][:, 0:TT], AF.Sigmoid),
                 reads=[rbank], writes=[f"rr{m % 2}"])

            def wv_fn(e, m=m, bank=7 if m % 2 == 0 else 0):
                for k in range(NCH):
                    op = e.matmul(pb[bank][:, 0:TT], wv_sb[:, D * k + 128 * m: D * k + 128 * m + 128],
                                  kact[:, C(k)], start=(k == 0), stop=(k == NCH - 1))
                return op
            P.ev("pe", wv_fn, reads=["kact", "wv_sb"], writes=[vbank2])
            P.ev("dv", lambda e, m=m, bank=7 if m % 2 == 0 else 0: e.tensor_mul(
                     fring[:, TT * (m % 2): TT * (m % 2) + TT],
                     r_sb[:, TT * (m % 2): TT * (m % 2) + TT], pb[bank][:, 0:TT]),
                 reads=[vbank2, f"rr{m % 2}"], writes=[f"fr{m % 2}"])
            P.ev("pl", lambda e, m=m, x=x: e.tensor_add(
                     x[:, R(m)], x[:, R(m)], fring[:, TT * (m % 2): TT * (m % 2) + TT]),
                 reads=[f"fr{m % 2}"], writes=[xsn])

    def plan_E_head(t):
        p, x, xsn, xnv, xnn, hsv, hsn, hbv, hbn, ruv, run, rpv, rpn = mvars(t)
        r0, r1, r2, r3 = rr[0]
        def s2a_fn(e):
            for c in range(NCH):
                op = e.matmul(pb[5][0:1, 0:TT], ones1b[:], sqbf1[:, C(c)],
                              start=(c == 0), stop=(c == NCH - 1))
            return op
        P.ev("pe", s2a_fn, reads=["sqbf", "ones1b"], writes=["pb5"])
        P.ev("dv", lambda e: e.tensor_scalar(r0[0:1, 0:TT], pb[5][0:1, 0:TT], 1.0 / D, None, ALU.mult),
             reads=["pb5"], writes=["r0p0"])

        P.ev("ac", lambda e: e.activation(r2[0:1, 0:TT], r0[0:1, 0:TT], AF.Sqrt, bias=cvec[0:1, 1:2]),
             reads=["r0p0", "cvec"], writes=["r2p0"])
        P.ev("dv", lambda e: e.reciprocal(r0[0:1, 0:TT], r2[0:1, 0:TT]),
             reads=["r2p0"], writes=["r0p0"])

        P.ev("pe", lambda e: e.matmul(pb[5][:, 0:TT], onesk1[:], r0[0:1, 0:TT], start=True, stop=True),
             reads=["r0p0", "onesk1"], writes=["pb5"])
        P.ev("dv", lambda e: e.tensor_copy(rzbc[:], pb[5][:, 0:TT]), reads=["pb5"], writes=["rzbc"])

        def z_fn(e, av=a_sb2[t % 2]):
            for c in range(NCH):
                op = e.scalar_tensor_tensor(zbf[:, C(c)], av[:, C(c)], wffn_v[:, c:c + 1],
                                            rzbc[:], ALU.mult, ALU.mult)
            return op
        P.ev("dv", z_fn, reads=[f"a_sb{t % 2}", "rzbc", "wffn_v"], writes=["zbf"])

    ffn_banks = [0, 1, 6, 7]

    def plan_E_mms(t):
        NG = len(FGRP)
        w2q = []

        def plan_w2(m, fw):
            def w2_fn(e, fw=fw, m=m):
                s = m % 3
                for dm in range(NCH):
                    bk = ffn_banks[dm // 2]
                    col = (dm % 2) * TT
                    op = e.matmul(pb[bk][:, col:col + TT],
                                  w2r[0:fw, D * s + 128 * dm: D * s + 128 * dm + 128],
                                  gvbf[0:fw, TT * (m % 3): TT * (m % 3) + TT],
                                  start=(m == 0 and dm % 2 == 0), stop=(m == len(FCH) - 1),
                                  skip_group_check=True)
                return op
            P.ev("pe", w2_fn, reads=[f"gv{m % 3}", f"w2r{m % 3}"],
                 writes=[PBN[b] for b in ffn_banks])
            if m + 3 < len(FCH):
                P.ev("ac", stream_w2(m + 3), reads=["w2s"], writes=[f"w2r{(m + 3) % 3}"], dma=1)

        for gi, (f0g, fwg, ms) in enumerate(FGRP):
            gb = (NG * t + gi) % 2
            for m in ms:
                f0, fw = FCH[m]
                off = f0 - f0g
                gbank = 2 + m % 2
                vbank = 4 + m % 2
                gband = PBN[gbank]
                vband = PBN[vbank]

                def gate_fn(e, fw=fw, off=off, gb=gb, bank=gbank, fwg=fwg):
                    for k in range(NCH):
                        op = e.matmul(pb[bank][0:fw, 0:TT],
                                      w1g[gb][:, fwg * k + off: fwg * k + off + fw],
                                      zbf[:, C(k)], start=(k == 0), stop=(k == NCH - 1))
                    return op
                P.ev("pe", gate_fn, reads=["zbf", f"w1g{gb}"], writes=[gband])

                def val_fn(e, fw=fw, off=off, gb=gb, bank=vbank, fwg=fwg):
                    for k in range(NCH):
                        op = e.matmul(pb[bank][0:fw, 0:TT],
                                      w3g[gb][:, fwg * k + off: fwg * k + off + fw],
                                      zbf[:, C(k)], start=(k == 0), stop=(k == NCH - 1))
                    return op
                P.ev("pe", val_fn, reads=["zbf", f"w3g{gb}"], writes=[vband])

                def sv_evac(e, fw=fw, m=m, gbank=gbank, vbank=vbank):
                    e.activation(gssc[0:fw, TT * (m % 2): TT * (m % 2) + TT],
                                 pb[gbank][0:fw, 0:TT], AF.Silu)
                    return e.activation(vssc[0:fw, TT * (m % 2): TT * (m % 2) + TT],
                                        pb[vbank][0:fw, 0:TT], AF.Identity)
                P.ev("ac", sv_evac, reads=[gband, vband], writes=[f"gs{m % 2}", f"vs{m % 2}"])
                P.ev("pl", lambda e, fw=fw, m=m: e.tensor_mul(
                         gvbf[0:fw, TT * (m % 3): TT * (m % 3) + TT],
                         gssc[0:fw, TT * (m % 2): TT * (m % 2) + TT],
                         vssc[0:fw, TT * (m % 2): TT * (m % 2) + TT]),
                     reads=[f"gs{m % 2}", f"vs{m % 2}"], writes=[f"gv{m % 3}"])

                w2q.append((m, fw))
                if len(w2q) > 2:
                    plan_w2(*w2q.pop(0))
            nidx = NG * t + gi + 2
            if nidx < NG * NT:
                nb = nidx % 2
                P.ev("sy", stream_w13(nb, nidx % NG), reads=["w1s", "w3s"],
                     writes=[f"w1g{nb}", f"w3g{nb}"], dma=2)
        while w2q:
            plan_w2(*w2q.pop(0))
        if t + 1 < NT:
            for m in range(3):
                P.ev("ac", stream_w2(m), reads=["w2s"], writes=[f"w2r{m % 3}"], dma=1)

    def plan_F(t):
        p, x, xsn, xnv, xnn, hsv, hsn, hbv, hbn, ruv, run, rpv, rpn = mvars(t)
        def fin_a(e, x=x, av=a_sb2[p]):
            for m in range(NCH):
                op = e.tensor_add(x[:, R(m)], x[:, R(m)], av[:, C(m)])
            return op
        P.ev("pl", fin_a, reads=[f"a_sb{p}"], writes=[xsn])

        def fin_fn(e, x=x):
            for m in range(NCH):
                bk = ffn_banks[m // 2]
                col = (m % 2) * TT
                op = e.tensor_add(x[:, R(m)], x[:, R(m)], pb[bk][:, col:col + TT])
            return op
        P.ev("dv", fin_fn, reads=[PBN[b] for b in ffn_banks], writes=[xsn])

        def store_fn(e, ic, t=t, x=x):
            return ic(e.dma_start(
                out=outT[:, TT * t: TT * t + TT].rearrange("(c p) w -> p c w", p=128),
                in_=x[:].rearrange("p (c w) -> p c w", w=ET)[:, :, 1:ET]))
        P.ev("pl", store_fn, reads=[xsn], writes=[f"st{t}"], dma=1)

        if t + 2 < NT:
            P.ev("sy", xload(t + 2), writes=[f"xs{t % 2}"], dma=1)

    plan_B(0)
    for t in range(NT):
        plan_C(t)
        plan_D_wk(t)
        plan_E_head(t)
        plan_D_rv(t)
        if t + 1 < NT:
            plan_B(t + 1)
        plan_E_mms(t)
        plan_F(t)

    import contextlib
    es = contextlib.ExitStack()
    sems = {k: es.enter_context(nc.semaphore(f"sem_{k}"))
            for k in list(P.cnt.keys())}
    block = es.enter_context(nc.Block())

    def replay(engine_key, e):
        waited = {}
        for waits, fn, dma, post, own in P.events[engine_key]:
            for sk, v in waits:
                if waited.get(sk, -1) < v:
                    e.wait_ge(sems[sk], v)
                    waited[sk] = v
            if dma:
                amt = 1 if dma == "coll" else 16
                ic = lambda inst, s=own, a=amt: inst.then_inc(sems[s], a)
                fn(e, ic)
            else:
                last = fn(e)
                if engine_key == "pe":
                    last.then_inc(sems["pe"], 1)
                else:
                    e.drain().then_inc(sems[engine_key], 1)
        if engine_key == "ac":
            for i in range(P.NQA):
                if P.cnt[f"qa{i}"]:
                    e.wait_ge(sems[f"qa{i}"], P.cnt[f"qa{i}"])
        if engine_key == "pl":
            for i in range(P.NQG):
                if P.cnt[f"qg{i}"]:
                    e.wait_ge(sems[f"qg{i}"], P.cnt[f"qg{i}"])
            if P.cnt["qc"]:
                e.wait_ge(sems["qc"], P.cnt["qc"])
        if engine_key == "sy":
            for i in range(P.NQS):
                if P.cnt[f"qs{i}"]:
                    e.wait_ge(sems[f"qs{i}"], P.cnt[f"qs{i}"])

    with es:
        @block.tensor
        def _(tensor):
            replay("pe", tensor)

        @block.vector
        def _(vector):
            replay("dv", vector)

        @block.scalar
        def _(scalar):
            replay("ac", scalar)

        @block.gpsimd
        def _(gpsimd):
            replay("pl", gpsimd)

        @block.sync
        def _(sync):
            replay("sy", sync)

    return nc


def kernel(L0_attn_x, L0_attn_log_lambda, L0_attn_B_w, L0_attn_C_w, L0_attn_D_skip,
           L0_attn_ssm_norm_w, L0_attn_ffn_norm_w, L0_attn_w1, L0_attn_w2, L0_attn_w3,
           L0_ffn_W_r, L0_ffn_W_k, L0_ffn_W_v, L0_ffn_mu_r, L0_ffn_mu_k):
    x = np.asarray(L0_attn_x, dtype=np.float32)
    B, T, _ = x.shape
    f32 = lambda a: np.ascontiguousarray(np.asarray(a, dtype=np.float32))
    tr = lambda a: np.ascontiguousarray(np.asarray(a, dtype=np.float32).T)
    v8 = lambda a: np.ascontiguousarray(np.asarray(a, dtype=np.float32).reshape(NCH, 128).T)

    shared = dict(
        wrT=tr(L0_ffn_W_r), wkT=tr(L0_ffn_W_k), wvT=tr(L0_ffn_W_v),
        bwT=tr(L0_attn_B_w), cwT=tr(L0_attn_C_w),
        w1T=tr(L0_attn_w1), w3T=tr(L0_attn_w3), w2T=tr(L0_attn_w2),
        mu_r8=v8(L0_ffn_mu_r), mu_k8=v8(L0_ffn_mu_k), dsk8=v8(L0_attn_D_skip),
        wssm8=v8(L0_attn_ssm_norm_w), wffn8=v8(L0_attn_ffn_norm_w),
        loglam2=np.ascontiguousarray(np.asarray(L0_attn_log_lambda, np.float32).reshape(2, 128).T),
        tI128=np.ascontiguousarray(np.broadcast_to(np.arange(1, TT + 1, dtype=np.float32), (128, TT))),
    )

    in_maps = []
    for core in range(8):
        b, ch = core // 4, core % 4
        s = ch * TC
        if s == 0:
            xe = np.concatenate([np.zeros((1, D), np.float32), x[b, 0:TC, :]], axis=0)
        else:
            xe = x[b, s - 1: s + TC, :]
        m = dict(shared)
        m["xT"] = np.ascontiguousarray(xe.T)
        cexpo = np.empty((128, 8), np.float32)
        csel = np.zeros((128, 8), np.float32)
        for j in range(4):
            for c in range(2):
                if j < ch:
                    cexpo[:, 2 * j + c] = 4096.0 * (ch - 1 - j)
                    csel[:, 2 * j + c] = 1.0
                else:
                    cexpo[:, 2 * j + c] = 1e9
        m["cexpo8"] = cexpo
        m["csel8"] = csel
        in_maps.append(m)

    nc = build_program()
    res = run_bass_kernel_spmd(nc, in_maps, core_ids=list(range(8)))
    out = np.empty((B, T, D), np.float32)
    for core in range(8):
        b, ch = core // 4, core % 4
        out[b, ch * TC:(ch + 1) * TC, :] = res.results[core]["outT"].T
    return out
